# version 24
# baseline (speedup 1.0000x reference)
import numpy as np
import ml_dtypes
from contextlib import ExitStack

import concourse.bass as bass
import concourse.mybir as mybir
from concourse.bass_utils import run_bass_kernel_spmd

F32 = mybir.dt.float32
BF16 = mybir.dt.bfloat16
AF = mybir.ActivationFunctionType
ALU = mybir.AluOpType

NCORE = 8
B = 2
SEQ = 8192
TOK = 2048
HALO = 256
HTOK = TOK + 2 * HALO
D = 1024
NT = 16
FC = 8
H = 16
DH = 64
KT = 20
PADW = H * 65
DFF = 4096
NE = 8
DFFE = 2048
EPS = 1e-5
SHIFT = 80.0
Q0 = HALO


def _strip_range(m):
    q0 = max(128 * m - 256, Q0)
    q1 = min(128 * m + 384, Q0 + TOK)
    return q0, q1


class MK:
    def __init__(self, taps=()):
        self.taps = tuple(taps)
        nc = bass.Bass("TRN2", target_bir_lowering=False)
        self.nc = nc
        self.cT = self.cV = self.cS = self.cQ = self.cG = 0
        self.last_wait = {}
        self.pbf_rd = 0
        self.pf32_rd = 0
        self.out_toks = []

        dt = nc.dram_tensor
        self.x0 = dt("x0", [HTOK, D], F32, kind="ExternalInput").ap()
        self.x0p = dt("x0p", [HTOK, PADW], F32, kind="ExternalInput").ap()
        self.x0f = dt("x0f", [D, HTOK], F32, kind="ExternalInput").ap()
        self.mski = dt("mski", [KT, 128, 640], BF16, kind="ExternalInput").ap()
        self.idbi = dt("idbi", [128, 128], BF16, kind="ExternalInput").ap()
        self.idfi = dt("idfi", [128, 128], F32, kind="ExternalInput").ap()
        self.W1p = dt("W1p", [16, 128, 2048], F32, kind="ExternalInput").ap()
        self.b1 = dt("b1", [DFF], F32, kind="ExternalInput").ap()
        self.W2p = dt("W2p", [FC, 128, 4096], F32, kind="ExternalInput").ap()
        self.b2 = dt("b2", [D], F32, kind="ExternalInput").ap()
        self.g1 = dt("g1", [D], F32, kind="ExternalInput").ap()
        self.c1 = dt("c1", [D], F32, kind="ExternalInput").ap()
        self.g2 = dt("g2", [D], F32, kind="ExternalInput").ap()
        self.c2 = dt("c2", [D], F32, kind="ExternalInput").ap()
        self.Wg = dt("Wg", [NE, D], F32, kind="ExternalInput").ap()
        self.bg = dt("bg", [NE], F32, kind="ExternalInput").ap()
        self.w1tt = dt("w1tt", [NE, FC, 128, 2048], F32, kind="ExternalInput").ap()
        self.cdram = dt("cdram", [NE, TOK], BF16, kind="Internal").ap()
        self.w2tt = dt("w2tt", [NE, FC, 128, 1024], F32, kind="ExternalInput").ap()
        self.out = dt("out", [TOK, D], F32, kind="ExternalOutput").ap()
        self.tap_out = {}
        for t in self.taps:
            self.tap_out[t] = dt("tap_" + t, [TOK, D], F32, kind="ExternalOutput").ap()

        self.ctx = ExitStack()
        sem = lambda n: self.ctx.enter_context(nc.semaphore(n))
        self.sT = sem("sT")
        self.sV = sem("sV")
        self.sS = sem("sS")
        self.sGP = sem("sGP")
        self.cGP = 0
        self.NGL = 20
        self.NQL = 8
        self.gsems = [sem(f"g{i}") for i in range(self.NGL)]
        self.qsems = [sem(f"q{i}") for i in range(self.NQL)]
        self.gcnt = [0] * self.NGL
        self.qcnt = [0] * self.NQL
        self.grr = 0
        self.qrr = 0

        sbl = lambda n, shp, dtp: self.ctx.enter_context(
            nc.sbuf_tensor(n, shp, dtp, side="left"))
        self.x0s = sbl("x0s", [128, NT, D], F32)
        self.idb = sbl("idb", [128, 128], BF16)
        self.idf = sbl("idf", [128, 128], F32)
        self.b1s = sbl("b1s", [128, 32], F32)
        self.b2c = sbl("b2c", [128, FC], F32)
        self.WgT = sbl("WgT", [128, FC, NE], BF16)
        self.bgr = sbl("bgr", [128, NE], F32)
        self.logits = sbl("logits", [128, NT, NE], F32)
        self.cvals = sbl("cvals", [128, NT, NE], F32)
        self.stats = sbl("stats", [128, NT, 2], F32)
        self.negm = sbl("negm", [128, NT], F32)
        self.inv = sbl("inv", [128, NT], F32)
        self.bn6 = sbl("bn6", [128, 12], F32)
        self.gsc = sbl("gsc", [128, 4, NE], F32)
        self.gs1 = sbl("gs1", [128, 4], F32)
        self.gsc12 = sbl("gsc12", [128, NT, NE], F32)
        self.negm1 = sbl("negm1", [128, NT], F32)
        self.ge1 = sbl("ge1", [128, NT, NE], F32)
        self.gmk = sbl("gmk", [128, NT, NE], F32)
        self.gw8 = sbl("gw8", [128, NT, NE], F32)
        self.gs2 = sbl("gs2", [128, NT], F32)
        self.consts = sbl("consts", [128, 2], F32)
        nc.const_aps.register(F32, -SHIFT, self.consts[:, 0:1])
        nc.const_aps.register(F32, EPS, self.consts[:, 1:2])

    def tmark(self, ins):
        self.cT += 1
        ins.then_inc(self.sT, 1)
        return self.cT

    def vmark(self, ins):
        self.cV += 1
        ins.then_inc(self.sV, 1)
        return self.cV

    def smark(self, ins):
        self.cS += 1
        ins.then_inc(self.sS, 1)
        return self.cS

    def vop(self, name, *args, **kw):
        self.wait("vector", self.sV, self.cV)
        return self.vmark(getattr(self.nc.vector, name)(*args, **kw))

    def vop_nw(self, name, *args, **kw):
        return self.vmark(getattr(self.nc.vector, name)(*args, **kw))

    def sop(self, name, *args, **kw):
        self.wait("scalar", self.sS, self.cS)
        return self.smark(getattr(self.nc.scalar, name)(*args, **kw))

    def sop_nw(self, name, *args, **kw):
        return self.smark(getattr(self.nc.scalar, name)(*args, **kw))

    def gop_nw(self, name, *args, **kw):
        self.cGP += 1
        getattr(self.nc.gpsimd, name)(*args, **kw).then_inc(self.sGP, 1)
        return self.cGP

    def qdma(self, out, in_):
        i = self.qrr
        self.qrr = (self.qrr + 1) % self.NQL
        sm = self.qsems[i]
        self.wait("sync", sm, self.qcnt[i])
        self.qcnt[i] += 16
        self.nc.sync.dma_start(out=out, in_=in_).then_inc(sm, 16)
        return (sm, self.qcnt[i])

    def gdma(self, out, in_, accum=False):
        i = self.grr
        self.grr = (self.grr + 1) % self.NGL
        sm = self.gsems[i]
        self.wait("gpsimd", sm, self.gcnt[i])
        self.gcnt[i] += 16
        kw = {"accum_op": ALU.add} if accum else {}
        self.nc.gpsimd.dma_start(out=out, in_=in_, **kw).then_inc(sm, 16)
        return (sm, self.gcnt[i])

    def waitd(self, eng, tok):
        if isinstance(tok, list):
            for t in tok:
                self.waitd(eng, t)
            return
        if tok == 0 or tok is None:
            return
        sm, v = tok
        self.wait(eng, sm, v)

    def wait(self, eng, sem, val):
        if val is None or val <= 0:
            return
        key = (eng, sem.name if hasattr(sem, "name") else id(sem))
        if self.last_wait.get(key, -1) >= val:
            return
        self.last_wait[key] = val
        getattr(self.nc, eng).wait_ge(sem, val)

    def barrier(self):
        for e in ("tensor", "vector", "scalar", "sync", "gpsimd"):
            self.wait(e, self.sT, self.cT)
            self.wait(e, self.sV, self.cV)
            self.wait(e, self.sS, self.cS)
            self.wait(e, self.sGP, self.cGP)
            for i in range(self.NQL):
                self.wait(e, self.qsems[i], self.qcnt[i])
            for i in range(self.NGL):
                self.wait(e, self.gsems[i], self.gcnt[i])

    def pbf_write_guard(self):
        self.wait("tensor", self.sV, self.pbf_rd)

    def pf32_write_guard(self):
        self.wait("tensor", self.sV, self.pf32_rd)

    def phase0_loads(self):
        nc = self.nc
        self.m_idb = self.qdma(self.idb[:], self.idbi)
        m_id = self.qdma(self.idf[:], self.idfi)
        self.m_idf = m_id
        self.m_bgr = self.qdma(self.bgr[:], bass.AP(self.bg.tensor, 0, [[0, 128], [1, NE]]))
        self.m_bload = [
            self.qdma(self.b1n[0:32, :], bass.AP(self.b1.tensor, 0, [[128, 32], [1, 128]])),
            self.qdma(self.b2n[0:8, :], bass.AP(self.b2.tensor, 0, [[128, 8], [1, 128]]))]
        self.m_msk_a = self.qdma(
            self.msk[:, 0:6, :],
            bass.AP(self.mski.tensor, 0, [[640, 128], [128 * 640, 6], [1, 640]]))
        self.m_x0s_a = self.qdma(
            self.x0s[:, 0:8, :],
            bass.AP(self.x0.tensor, Q0 * D, [[D, 128], [128 * D, 8], [1, D]]))
        self.m_msk_b = self.qdma(
            self.msk[:, 6:KT, :],
            bass.AP(self.mski.tensor, 6 * 128 * 640,
                    [[640, 128], [128 * 640, KT - 6], [1, 640]]))
        self.m_x0s_b = self.qdma(
            self.x0s[:, 8:NT, :],
            bass.AP(self.x0.tensor, (Q0 + 8 * 128) * D, [[D, 128], [128 * D, 8], [1, D]]))
        self.m_x0s = [self.m_x0s_a, self.m_x0s_b]
        self.m_wgn = self.gdma(self.wgn[0:NE, :], self.Wg)
        self.m_x0fm = [None] * FC
        self.m_x0fm[0] = self.gdma(
            self.x0fm[:, 0, :],
            bass.AP(self.x0f.tensor, 0, [[HTOK, 128], [1, HTOK]]))
        self.m_x0tm = [self.gdma(
            bass.AP(self.x0tm, m * PADW, [[KT * PADW, 128], [1, PADW]]),
            bass.AP(self.x0p.tensor, m * 128 * PADW, [[PADW, 128], [1, PADW]]))
            for m in range(KT)]
        for fc in range(1, FC):
            self.m_x0fm[fc] = self.gdma(
                self.x0fm[:, fc, :],
                bass.AP(self.x0f.tensor, fc * 128 * HTOK, [[HTOK, 128], [1, HTOK]]))

        self.vop("memset", self.consts[:, 0:1], -SHIFT)
        self.m_ones = self.vop("memset", self.consts[:, 1:2], EPS)

        self.waitd("tensor", self.m_bload)
        self.waitd("tensor", m_id)
        self.waitd("tensor", self.m_idb)
        self.waitd("tensor", self.m_wgn)
        nc.tensor.transpose(self.pf32[:, 0:32], self.b1n[0:32, :], self.idf[0:32, 0:32])
        t1 = self.tmark(nc.tensor.transpose(self.pf32[:, 32:40], self.b2n[0:8, :],
                                            self.idf[0:8, 0:8]))
        for kc in range(FC):
            ins = nc.tensor.transpose(self.pbf[:, 8 * kc:8 * kc + 8],
                                      self.wgn[0:NE, 128 * kc:128 * kc + 128],
                                      self.idb[0:NE, 0:NE])
        t2 = self.tmark(ins)
        self.wait("vector", self.sT, t1)
        self.vop("tensor_copy", self.b1s[:], self.pf32[:, 0:32])
        self.pf32_rd = self.vop("tensor_copy", self.b2c[:], self.pf32[:, 32:40])
        self.m_bias_rd = self.pf32_rd
        self.wait("vector", self.sT, t2)
        self.m_params = self.vop("tensor_copy", 
            self.WgT[:], bass.AP(self.pbf, 0, [[1024, 128], [8, FC], [1, 8]]))
        self.pbf_rd = self.m_params

    def phase2_attn(self):
        nc = self.nc
        strip_last_pv = {}
        exp_mark = {}
        mask_mark = {}
        pv_mark = {}
        evac_mark = {}
        trans_mark = {}
        stt_mark = {}
        RING = 8
        LAG = 6
        self.wait("tensor", self.sV, self.m_params)
        gg = 0

        def do_pv(h, T):
            nonlocal gg
            grp = T // 4
            tloc = T % 4
            gb = h * 4 + grp
            self.wait("tensor", self.sGP, mask_mark[h * KT + T + 4])
            self.waitd("tensor", [self.m_x0tm[mm] for mm in range(T, T + 5)])
            ppv = self.ppv[grp % 2]
            if tloc == 0:
                self.wait("tensor", self.sV, evac_mark.get(gb - 2, 0))
            for i, mm in enumerate(range(T, T + 5)):
                sl = (h * KT + mm) % RING
                sq0, _ = _strip_range(mm)
                qc = (Q0 + 128 * T) - sq0
                ins = nc.tensor.matmul(
                    ppv[0:65, 128 * tloc:128 * tloc + 128],
                    self.x0tm[:, mm, 65 * h:65 * h + 65],
                    self.strb[:, sl, qc:qc + 128],
                    start=(i == 0), stop=(i == 4))
            pv_mark[gb] = pvm = self.tmark(ins)
            for mm in range(T, T + 5):
                k = h * KT + mm
                strip_last_pv[k] = max(strip_last_pv.get(k, 0), pvm)
            if tloc == 3:
                self.wait("vector", self.sT, pv_mark[gb])
                if gg >= 2:
                    self.wait("vector", self.sT, trans_mark.get(gg - 2, 0))
                evac_mark[gb] = self.vop_nw(
                    "tensor_copy", self.attnh[0:65, gg % 2, :], ppv[0:65, :])
                self.wait("tensor", self.sV, evac_mark[gb])
                if gg >= 2:
                    self.wait("tensor", self.sV, stt_mark.get(gg - 2, 0))
                ptm = self.ptm[gg % 2]
                for tl in range(4):
                    ins = nc.tensor.transpose(
                        ptm[:, 65 * tl:65 * tl + 65],
                        self.attnh[0:65, gg % 2, 128 * tl:128 * tl + 128],
                        self.idf[0:65, 0:65])
                trans_mark[gg] = self.tmark(ins)
                self.wait("vector", self.sT, trans_mark[gg])
                self.waitd("vector", self.m_x0s_a if grp < 2 else self.m_x0s_b)
                self.wait("vector", self.sV, stt_mark.get(gg - 1, 0))
                rm = self.vop_nw("reciprocal", self.invz[:, 0:4],
                                 bass.AP(ptm, 64, [[512, 128], [65, 4]]))
                self.wait("vector", self.sV, rm)
                for tl in range(4):
                    T2 = 4 * grp + tl
                    ins = self.vop_nw(
                        "scalar_tensor_tensor",
                        self.x0s[:, T2, 64 * h:64 * h + 64],
                        ptm[:, 65 * tl:65 * tl + 64],
                        self.invz[:, tl:tl + 1],
                        self.x0s[:, T2, 64 * h:64 * h + 64],
                        ALU.mult, ALU.add)
                stt_mark[gg] = ins
                gg += 1

        for h in range(H):
            f = h // 2
            self.waitd("tensor", self.m_x0fm[f])
            rows = slice(64 * (h % 2), 64 * (h % 2) + 64)
            for m in range(KT):
                gidx = h * KT + m
                slot = gidx % RING
                q0, q1 = _strip_range(m)
                w = q1 - q0
                ps = self.ps2[m % 2]
                if gidx >= 2:
                    self.wait("tensor", self.sS, exp_mark[gidx - 2])
                kap = self.x0fm[rows, f, 128 * m:128 * m + 128]
                c0 = 0
                while c0 < w:
                    n = min(512, w - c0)
                    ins = nc.tensor.matmul(
                        ps[:, c0:c0 + n], kap,
                        self.x0fm[rows, f, q0 + c0:q0 + c0 + n],
                        start=True, stop=True)
                    c0 += n
                qk_m = self.tmark(ins)
                self.wait("scalar", self.sV, self.m_ones)
                self.wait("scalar", self.sT, qk_m)
                if gidx >= RING:
                    self.wait("scalar", self.sT, strip_last_pv.get(gidx - RING, 0))
                    self.wait("scalar", self.sGP, mask_mark.get(gidx - RING, 0))
                exp_mark[gidx] = self.sop_nw(
                    "activation", self.strb[:, slot, 0:w], ps[:, 0:w],
                    AF.Exp, bias=-SHIFT, scale=1.0)
                self.waitd("gpsimd", self.m_msk_a if m < 6 else self.m_msk_b)
                self.wait("gpsimd", self.sS, exp_mark[gidx])
                if gidx >= RING:
                    self.wait("gpsimd", self.sT, strip_last_pv.get(gidx - RING, 0))
                mask_mark[gidx] = self.gop_nw(
                    "tensor_mul", self.strb[:, slot, 0:w],
                    self.strb[:, slot, 0:w], self.msk[:, m, 0:w])
                T = m - LAG
                if 0 <= T < NT:
                    do_pv(h, T)
            for m in range(KT, KT + LAG):
                T = m - LAG
                if 0 <= T < NT:
                    do_pv(h, T)

    def layernorm(self, gb=None, tiles=None):
        nc = self.nc
        tiles = list(range(NT)) if tiles is None else list(tiles)
        for t in tiles:
            self.vop("bn_stats", self.bn6[:, 0:6], self.x0s[:, t, 0:512])
            self.vop("bn_stats", self.bn6[:, 6:12], self.x0s[:, t, 512:1024])
            self.vop("bn_aggr", self.stats[:, t, :], self.bn6[:])
        t0, nt = tiles[0], len(tiles)
        m1 = self.vop("tensor_scalar_mul", 
            self.negm[:, t0:t0 + nt],
            bass.AP(self.stats, t0 * 2, [[NT * 2, 128], [2, nt]]), -1.0)
        self.wait("scalar", self.sV, m1)
        m2 = self.sop("activation", 
            self.inv[:, t0:t0 + nt],
            bass.AP(self.stats, t0 * 2 + 1, [[NT * 2, 128], [2, nt]]),
            AF.Sqrt, bias=EPS, scale=1.0)
        self.wait("vector", self.sS, m2)
        self.vop("reciprocal", self.inv[:, t0:t0 + nt], self.inv[:, t0:t0 + nt])
        for t in tiles:
            ins = self.vop("tensor_scalar", 
                self.x0s[:, t, :], self.x0s[:, t, :],
                self.negm[:, t:t + 1], self.inv[:, t:t + 1], ALU.add, ALU.mult)
            if gb is not None:
                self.vop("tensor_mul", self.x0s[:, t, :], self.x0s[:, t, :],
                                     self.gbr4[:, gb, :])
                ins = self.vop("tensor_add", self.x0s[:, t, :], self.x0s[:, t, :],
                                           self.gbr4[:, gb + 1, :])
        return ins

    def fm_build(self, dst, pfm, pfm_rd0=0, tiles=None, dep=0):
        nc = self.nc
        self.wait("tensor", self.sV, dep)
        pfm_rd = pfm_rd0
        last = 0
        for t in (range(NT) if tiles is None else tiles):
            for g2 in range(2):
                self.wait("tensor", self.sV, pfm_rd)
                for j4 in range(4):
                    j = 4 * g2 + j4
                    ins = nc.tensor.transpose(
                        pfm[:, 512 * g2 + 128 * j4:512 * g2 + 128 * j4 + 128],
                        self.x0s[:, t, 128 * j:128 * j + 128], self.idf[:])
                tm = self.tmark(ins)
                self.wait("vector", self.sT, tm)
                ins2 = self.vop("tensor_copy", 
                    bass.AP(dst, (4 * g2 * TOK + 128 * t),
                            [[FC * TOK, 128], [TOK, 4], [1, 128]]),
                    pfm[:, 512 * g2:512 * g2 + 512])
                pfm_rd = ins2
            last = pfm_rd
        return last

    def phase_ffn(self):
        nc = self.nc
        self.wait("scalar", self.sV, self.m_bias_rd)
        relu_mark = {}
        ff1_mark = {}
        ff2_mark = {}
        ev2_mark = {}
        flip_mark = {}
        add_mark = {}
        prev_w1_user = {}
        prev_w2_user = {}
        self.ln2m = {}
        self.u1m = {}
        mig_g = 0
        dc_g = 0
        for half in range(2):
            for grp in range(16):
                gslot = grp % 2
                self.wait("gpsimd", self.sT, prev_w1_user.get(gslot, 0))
                w1d = self.gdma(
                    self.w1buf[:, gslot, :, :],
                    bass.AP(self.W1p.tensor, grp * 128 * 2048,
                            [[2048, 128], [1, 2048]]))
                self.waitd("tensor", w1d)
                for mi in range(2):
                    hb = self.hb[mig_g % 2]
                    if mig_g >= 2:
                        self.wait("tensor", self.sS, relu_mark[mig_g - 2])
                    mig = 2 * grp + mi
                    for nb in range(2):
                        self.wait("tensor", self.sV, self.ybf_half[2 * half + nb])
                        for ki in range(FC):
                            ins = nc.tensor.matmul(
                                hb[:, 512 * nb:512 * nb + 512],
                                self.w1buf[:, gslot, ki, 128 * mi:128 * mi + 128],
                                self.ybf[:, ki,
                                         1024 * half + 512 * nb:1024 * half + 512 * nb + 512],
                                start=(ki == 0), stop=(ki == FC - 1))
                    mmk = self.tmark(ins)
                    self.wait("scalar", self.sT, mmk)
                    relu_mark[mig_g] = self.sop_nw("activation", 
                        self.hsb[:, mig, :], hb[:, 0:1024], AF.Relu,
                        bias=self.b1s[:, mig:mig + 1], scale=1.0)
                    mig_g += 1
                prev_w1_user[gslot] = self.cT
            if half == 1:
                self.outer_ln_half(0)
            for dc in range(FC):
                dslot = dc % 2
                self.wait("gpsimd", self.sT, prev_w2_user.get(dslot, 0))
                w2d = self.gdma(
                    self.w2buf[:, dslot, :, :],
                    bass.AP(self.W2p.tensor, dc * 128 * 4096,
                            [[4096, 128], [1, 4096]]))
                fb = self.hb[dslot]
                self.waitd("tensor", w2d)
                self.wait("tensor", self.sS, relu_mark[mig_g - 1])
                if dc_g >= 2:
                    self.wait("tensor", self.sS, ev2_mark[dc_g - 2])
                elif half == 0 and dc < 2:
                    self.wait("tensor", self.sS, relu_mark[mig_g - 1])
                for nb in range(2):
                    for ji in range(32):
                        ins = nc.tensor.matmul(
                            fb[:, 512 * nb:512 * nb + 512],
                            self.w2buf[:, dslot, ji, :],
                            self.hsb[:, ji, 512 * nb:512 * nb + 512],
                            start=(ji == 0), stop=(ji == 31))
                ff2_mark[dc_g] = mmk = self.tmark(ins)
                prev_w2_user[dslot] = mmk
                self.wait("scalar", self.sT, mmk)
                if dc_g >= 1:
                    self.wait("scalar", self.sT, flip_mark[dc_g - 1])
                ev2_mark[dc_g] = self.sop_nw("activation", 
                    self.ffsb[:, 0, :], fb[:, 0:1024], AF.Identity,
                    bias=self.b2c[:, dc:dc + 1], scale=1.0)
                self.wait("tensor", self.sS, ev2_mark[dc_g])
                self.pbf_write_guard()
                for tl in range(8):
                    ins = nc.tensor.transpose(
                        self.pbf[:, 128 * tl:128 * tl + 128],
                        self.ffsb[:, 0, 128 * tl:128 * tl + 128], self.idb[:])
                flip_mark[dc_g] = self.tmark(ins)
                self.wait("vector", self.sT, flip_mark[dc_g])
                xap = bass.AP(self.x0s, (8 * half) * D + 128 * dc,
                              [[NT * D, 128], [D, 8], [1, 128]])
                add_mark[dc_g] = self.vop("tensor_add", 
                    xap, bass.AP(self.pbf, 0, [[1024, 128], [128, 8], [1, 128]]), xap)
                self.pbf_rd = add_mark[dc_g]
                dc_g += 1
            self.waitd("vector", self.mg_tok)
            self.ln2m[half] = self.layernorm(gb=2, tiles=range(8 * half, 8 * half + 8))

    def outer_ln_half(self, half):
        self.wait("gpsimd", self.sV, self.ln2m[half])
        xr = self.gdma(
            self.x0s[:, 8 * half:8 * half + 8, :],
            bass.AP(self.x0.tensor, (Q0 + 1024 * half) * D,
                    [[D, 128], [128 * D, 8], [1, D]]),
            accum=True)
        self.waitd("vector", xr)
        self.u1m[half] = self.layernorm(gb=None, tiles=range(8 * half, 8 * half + 8))

    def phase_gating(self, half):
        nc = self.nc
        t0 = 8 * half
        self.wait("tensor", self.sV, self.ubf_h0 if half == 0 else self.m_ubf)
        self.wait("tensor", self.sV, self.m_params)
        self.pf32_write_guard()
        for t in range(t0, t0 + 8):
            for f in range(FC):
                ins = nc.tensor.matmul(
                    self.pf32[:, 8 * t:8 * t + 8],
                    self.ubf[:, f, 128 * t:128 * t + 128],
                    self.WgT[:, f, :],
                    start=(f == 0), stop=(f == FC - 1))
        lg = self.tmark(ins)
        self.wait("vector", self.sT, lg)
        self.waitd("vector", self.m_bgr)
        self.pf32_rd = self.vop("tensor_add", 
            bass.AP(self.logits, t0 * NE, [[NT * NE, 128], [NE, 8], [1, NE]]),
            bass.AP(self.pf32, 8 * t0, [[512, 128], [NE, 8], [1, NE]]),
            bass.AP(self.bgr, 0, [[NE, 128], [0, 8], [1, NE]]))
        self.wait("vector", self.sV, self.cV)
        for t in range(t0, t0 + 8):
            self.vop_nw("max", self.gsc12[:, t, :], self.logits[:, t, :])
        mv = self.vop("tensor_scalar_mul",
            self.negm1[:, t0:t0 + 8],
            bass.AP(self.gsc12, t0 * NE, [[NT * NE, 128], [NE, 8]]), -1.0)
        self.wait("scalar", self.sV, mv)
        se = 0
        for t in range(t0, t0 + 8):
            se = self.sop_nw("activation",
                self.ge1[:, t, :], self.logits[:, t, :], AF.Exp,
                bias=self.negm1[:, t:t + 1], scale=1.0)
        self.wait("vector", self.sS, se)
        for t in range(t0, t0 + 8):
            self.vop_nw("tensor_single_scalar",
                self.gmk[:, t, :], self.logits[:, t, :],
                self.gsc12[:, t, 1:2], ALU.is_ge)
        self.wait("vector", self.sV, self.cV)
        for t in range(t0, t0 + 8):
            self.vop_nw("scalar_tensor_tensor",
                self.gw8[:, t, :], self.ge1[:, t, :], 1.0, self.gmk[:, t, :],
                ALU.mult, ALU.mult, accum_out=self.gs2[:, t:t + 1])
        iv = self.vop("reciprocal", self.gs2[:, t0:t0 + 8], self.gs2[:, t0:t0 + 8])
        self.wait("vector", self.sV, iv)
        ins = 0
        for t in range(t0, t0 + 8):
            ins = self.vop_nw("tensor_scalar",
                self.cvals[:, t, :], self.gw8[:, t, :],
                self.gs2[:, t:t + 1], None, ALU.mult)
        self.m_gate = ins
        self.wait("tensor", self.sV, self.m_gate)
        for qt in range(2 * half, 2 * half + 2):
            self.pf32_write_guard()
            for tl in range(4):
                t = 4 * qt + tl
                ins = nc.tensor.transpose(
                    self.pf32[0:NE, 128 * tl:128 * tl + 128],
                    bass.AP(self.cvals, t * NE, [[NT * NE, 128], [1, NE]]),
                    self.idf[:])
            tm = self.tmark(ins)
            self.wait("vector", self.sT, tm)
            ins = self.vop("tensor_copy", self.ctsb[0:NE, 512 * qt:512 * qt + 512],
                                        self.pf32[0:NE, 0:512])
            self.pf32_rd = ins
        self.wait("gpsimd", self.sV, self.pf32_rd)
        self.m_cdram[half] = self.gdma(
            bass.AP(self.cdram.tensor, 1024 * half, [[TOK, NE], [1, 1024]]),
            self.ctsb[0:NE, 1024 * half:1024 * half + 1024])

    def moe_init(self):
        self._silu = {}
        self._amul = {}
        self._hmm = {}
        self._w1slot = {}
        self.m_cdram = {}

    def stageA_load(self, e, g):
        slot2 = g % 2
        self.wait("gpsimd", self.sT, self._w1slot.get(slot2, 0))
        return self.gdma(
            bass.AP(self.w1T, slot2 * 2048, [[2 * 2048, 128], [1, 2048]]),
            bass.AP(self.w1tt.tensor, (e * FC + g) * 128 * 2048,
                    [[2048, 128], [1, 2048]]))

    def stageA_units(self, eidx, e, g, w1d, crep_d, tqs):
        nc = self.nc
        slot2 = g % 2
        self.waitd("tensor", w1d)
        for tq in tqs:
            qsl = tq % 2
            self.wait("tensor", self.sS,
                      max(self._silu.get((e, g, tq - 2), 0),
                          self._silu.get(("prev", qsl), 0)))
            self.wait("tensor", self.sV,
                      max(self._amul.get((e, g, tq - 2), 0),
                          self._amul.get(("prevu", qsl), 0)))
            for hu in range(2):
                hps = self.hg[qsl] if hu == 0 else self.hu[qsl]
                for kc in range(FC):
                    ins = nc.tensor.matmul(
                        hps[:],
                        bass.AP(self.w1T, slot2 * 2048 + kc * 256 + hu * 128,
                                [[2 * 2048, 128], [1, 128]]),
                        self.ubf[:, kc, 512 * tq:512 * tq + 512],
                        start=(kc == 0), stop=(kc == FC - 1))
            hm = self.tmark(ins)
            self._hmm[(e, g, tq)] = hm
            self._w1slot[slot2] = hm
            self.wait("scalar", self.sT, hm)
            self.wait("scalar", self.sV,
                      max(self._amul.get((e, g, tq - 2), 0),
                          self._amul.get(("prevu", qsl), 0)))
            sm = self.sop_nw("activation",
                self.sg[:, qsl, :], self.hg[qsl][:], AF.Silu,
                bias=0.0, scale=1.0)
            self._silu[(e, g, tq)] = sm
            self._silu[("prev", qsl)] = sm
            self.wait("vector", self.sS, sm)
            self.waitd("vector", crep_d)
            self.vop("tensor_mul",
                self.sg[:, qsl, :], self.sg[:, qsl, :],
                self.crep[:, eidx, 512 * tq:512 * tq + 512])
            am = self.vop("tensor_mul",
                self.a[:, eidx, g, 512 * tq:512 * tq + 512],
                self.sg[:, qsl, :], self.hu[qsl][:])
            self._amul[(e, g, tq)] = am
            self._amul[("prevu", qsl)] = am
        return am

    def moe_stageA(self, pair, eidx, tqs):
        e = 2 * pair + eidx
        ev = {0: self.stageA_load(e, 0)}
        ev[1] = self.stageA_load(e, 1)
        if tqs == (0, 1, 2, 3):
            self.waitd("gpsimd", self.m_cdram[0])
            self.waitd("gpsimd", self.m_cdram[1])
            self.wait("gpsimd", self.sV, self._amul.get(("ce", pair - 1, eidx), 0))
            crep_d = self.gdma(
                self.crep[:, eidx, :],
                bass.AP(self.cdram.tensor, e * TOK, [[0, 128], [1, TOK]]))
        elif tqs == (0, 1):
            self.waitd("gpsimd", self.m_cdram[0])
            crep_d = self.gdma(
                self.crep[:, eidx, 0:1024],
                bass.AP(self.cdram.tensor, e * TOK, [[0, 128], [1, 1024]]))
        else:
            self.waitd("gpsimd", self.m_cdram[1])
            crep_d = self.gdma(
                self.crep[:, eidx, 1024:2048],
                bass.AP(self.cdram.tensor, e * TOK + 1024, [[0, 128], [1, 1024]]))
        for g in range(1, FC + 1):
            if 2 <= g < FC:
                ev[g] = self.stageA_load(e, g)
            am = self.stageA_units(eidx, e, g - 1, ev[g - 1], crep_d, tqs)
        return am

    def moe_pass1(self):
        self.wait("tensor", self.sV, self.ubf_h0)
        for eidx in range(2):
            self.moe_stageA(0, eidx, (0, 1))

    def phase_moe(self):
        nc = self.nc
        w2t_user = {}
        oflip_prev = 0
        ob_user = {}
        self.wait("tensor", self.sV, self.m_ubf)

        for pair in range(4):
            for eidx in range(2):
                tqs = (2, 3) if pair == 0 else (0, 1, 2, 3)
                am = self.moe_stageA(pair, eidx, tqs)
                self._amul[("ce", pair, eidx)] = am
            a_ready = max(self._amul[(2 * pair + i, FC - 1, tq)]
                          for i in range(2) for tq in range(4))
            for half in range(2):
                for dc in range(FC):
                    dsl = dc % 2
                    w2toks = []
                    for eidx in range(2):
                        e = 2 * pair + eidx
                        self.wait("gpsimd", self.sT, w2t_user.get(dsl, 0))
                        w2toks.append(self.gdma(
                            self.w2t[:, dsl, eidx, :],
                            bass.AP(self.w2tt.tensor, (e * FC + dc) * 128 * 1024,
                                    [[1024, 128], [1, 1024]])))
                    self.waitd("tensor", w2toks)
                    self.wait("tensor", self.sV, a_ready)
                    self.wait("tensor", self.sS, ob_user.get(0, 0))
                    obp = self.obs[0]
                    for nb in range(2):
                        first = True
                        for eidx in range(2):
                            for jc in range(FC):
                                ins = nc.tensor.matmul(
                                    obp[:, 512 * nb:512 * nb + 512],
                                    self.w2t[:, dsl, eidx, 128 * jc:128 * jc + 128],
                                    self.a[:, eidx, jc,
                                           1024 * half + 512 * nb:1024 * half + 512 * nb + 512],
                                    start=first,
                                    stop=(eidx == 1 and jc == FC - 1))
                                first = False
                    om = self.tmark(ins)
                    w2t_user[dsl] = om
                    self.wait("scalar", self.sT, om)
                    self.wait("scalar", self.sT, oflip_prev)
                    oev = self.sop_nw("activation",
                        self.osb[:], obp[:, 0:1024], AF.Copy, bias=0.0, scale=1.0)
                    ob_user[0] = oev
                    self.wait("tensor", self.sS, oev)
                    self.pbf_write_guard()
                    for tl in range(8):
                        ins = nc.tensor.transpose(
                            self.pbf[:, 128 * tl:128 * tl + 128],
                            self.osb[:, 128 * tl:128 * tl + 128], self.idb[:])
                    ofl = self.tmark(ins)
                    oflip_prev = ofl
                    self.wait("vector", self.sT, ofl)
                    xap = bass.AP(self.x0s, (8 * half) * D + 128 * dc,
                                  [[NT * D, 128], [D, 8], [1, 128]])
                    self.pbf_rd = self.vop("tensor_add",
                        xap, bass.AP(self.pbf, 0, [[1024, 128], [128, 8], [1, 128]]),
                        xap)
                    if pair == 3 and dc == FC - 1 and "accpre" not in self.taps:
                        lnm = self.layernorm(gb=None, tiles=range(8 * half, 8 * half + 8))
                        self.wait("sync", self.sV, lnm)
                        mo = self.qdma(
                            bass.AP(self.out.tensor, half * 8 * 128 * D,
                                    [[D, 128], [128 * D, 8], [1, D]]),
                            self.x0s[:, 8 * half:8 * half + 8, :])
                        self.out_toks.append(mo)

    def tap(self, name):
        if name in self.taps:
            self.barrier()
            m = self.qdma(
                bass.AP(self.tap_out[name].tensor, 0, [[D, 128], [128 * D, NT], [1, D]]),
                self.x0s[:])
            self.waitd("sync", m)

    def build(self):
        nc = self.nc
        with self.ctx:
            attn_ctx = ExitStack()
            r = lambda n, shp, dtp: attn_ctx.enter_context(
                nc.sbuf_tensor(n, shp, dtp, side="right"))
            self.msk = r("msk", [128, KT, 640], BF16)
            self.x0tm = r("x0tm", [128, KT, PADW], BF16)
            self.x0fm = r("x0fm", [128, FC, HTOK], BF16)
            self.strb = r("strb", [128, 8, 640], BF16)
            self.attnh = r("attnh", [128, 2, 512], F32)
            self.invz = r("invz", [128, 4], F32)
            self.b1n = r("b1n", [128, 128], F32)
            self.b2n = r("b2n", [128, 128], F32)
            self.wgn = r("wgn", [128, D], BF16)

            p0_ctx = ExitStack()
            self.pbf = p0_ctx.enter_context(nc.psum_tensor("pbf0", [128, 1024], BF16))
            self.pf32 = p0_ctx.enter_context(nc.psum_tensor("pf0", [128, 512], F32))
            self.phase0_loads()
            p0_ctx.close()

            pa_ctx = ExitStack()
            pp = lambda n, shp: pa_ctx.enter_context(nc.psum_tensor(n, shp, F32))
            self.ps2 = [pp("ps0", [128, 640]), pp("ps1", [128, 640])]
            self.ppv = [pp("ppv0", [128, 512]), pp("ppv1", [128, 512])]
            self.ptm = [pp("ptm0", [128, 512]), pp("ptm1", [128, 512])]
            self.phase2_attn()
            self.tap("s")
            self.barrier()
            pa_ctx.close()
            self.tap("y")
            self.barrier()
            attn_ctx.close()

            gb_ctx = ExitStack()
            self.gbr4 = gb_ctx.enter_context(
                nc.sbuf_tensor("gbr4", [128, 4, D], F32, side="right"))
            ffn_ctx = ExitStack()
            r2 = lambda n, shp, dtp: ffn_ctx.enter_context(
                nc.sbuf_tensor(n, shp, dtp, side="right"))
            gtoks = [self.qdma(self.gbr4[:, i, :],
                            bass.AP(src.tensor, 0, [[0, 128], [1, D]]))
                  for i, src in enumerate((self.g1, self.c1, self.g2, self.c2))]
            self.mg_tok = gtoks
            self.waitd("vector", gtoks[:2])
            self.ybf = r2("ybf", [128, FC, TOK], BF16)
            self.hsb = r2("hsb", [128, 32, 1024], BF16)
            self.w1buf = r2("w1buf", [128, 2, 8, 256], BF16)
            self.w2buf = r2("w2buf", [128, 2, 32, 128], BF16)
            self.ffsb = r2("ffsb", [128, 1, 1024], BF16)
            pf_ctx = ExitStack()
            self.pbf = pf_ctx.enter_context(nc.psum_tensor("pbf2", [128, 1024], BF16))
            self.hb = [pf_ctx.enter_context(nc.psum_tensor("hb0", [128, 1024], F32)),
                       pf_ctx.enter_context(nc.psum_tensor("hb1", [128, 1024], F32))]
            pfmb = pf_ctx.enter_context(nc.psum_tensor("pfmb", [128, 1024], F32))
            self.pbf_rd = 0
            self.ybf_half = {}
            for qd in range(4):
                lnm = self.layernorm(gb=0, tiles=range(4 * qd, 4 * qd + 4))
                self.ybf_half[qd] = self.fm_build(
                    self.ybf, pfmb, tiles=range(4 * qd, 4 * qd + 4), dep=lnm)
            self.m_ybf = self.ybf_half[3]
            self.phase_ffn()
            self.barrier()
            pf_ctx.close()
            self.tap("zpre")
            ffn_ctx.close()
            self.tap("u1")
            self.barrier()
            gb_ctx.close()

            moe_ctx = ExitStack()
            r3 = lambda n, shp, dtp: moe_ctx.enter_context(
                nc.sbuf_tensor(n, shp, dtp, side="right"))
            self.ubf = r3("ubf", [128, FC, TOK], BF16)
            self.a = r3("a", [128, 2, FC, TOK], BF16)
            self.w1T = r3("w1T", [128, 2 * 2048], BF16)
            self.w2t = r3("w2t", [128, 2, 2, 1024], BF16)
            self.sg = r3("sg", [128, 2, 512], BF16)
            self.crep = r3("crep", [128, 2, TOK], BF16)
            self.ctsb = r3("ctsb", [128, TOK], BF16)
            self.osb = r3("osb", [128, 1024], BF16)
            pm_ctx = ExitStack()
            self.pbf = pm_ctx.enter_context(nc.psum_tensor("pbf3", [128, 1024], BF16))
            self.hg = [pm_ctx.enter_context(nc.psum_tensor("hg0", [128, 512], F32)),
                       pm_ctx.enter_context(nc.psum_tensor("hg1", [128, 512], F32))]
            self.hu = [pm_ctx.enter_context(nc.psum_tensor("hu0", [128, 512], F32)),
                       pm_ctx.enter_context(nc.psum_tensor("hu1", [128, 512], F32))]
            self.obs = [pm_ctx.enter_context(nc.psum_tensor("ob", [128, 1024], F32))]
            pg_ctx = ExitStack()
            self.pf32 = pg_ctx.enter_context(nc.psum_tensor("pf3", [128, 512], F32))
            self.pbf_rd = 0
            self.pf32_rd = 0
            ubf_q = {}
            for qd in range(2):
                ubf_q[qd] = self.fm_build(
                    self.ubf, self.obs[0], tiles=range(4 * qd, 4 * qd + 4),
                    dep=self.u1m[0])
            self.ubf_h0 = ubf_q[1]
            self.moe_init()
            self.phase_gating(0)
            self.outer_ln_half(1)
            self.moe_pass1()
            for qd in range(2, 4):
                ubf_q[qd] = self.fm_build(
                    self.ubf, self.obs[0], tiles=range(4 * qd, 4 * qd + 4),
                    dep=self.u1m[qd // 2])
            self.m_ubf = ubf_q[3]
            self.phase_gating(1)
            self.phase_moe()
            if "accpre" in self.taps:
                self.barrier()
                self.tap("accpre")
                self.layernorm(gb=None)
                m = self.qdma(
                    bass.AP(self.out.tensor, 0, [[D, 128], [128 * D, NT], [1, D]]),
                    self.x0s[:])
                self.out_toks.append(m)
            self.barrier()
            pg_ctx.close()
            pm_ctx.close()
            moe_ctx.close()
            self.waitd("sync", self.out_toks)
        return nc


def _prep_shared(inputs):
    W1 = np.asarray(inputs["L0_attn_W1"], dtype=np.float32)
    W2 = np.asarray(inputs["L0_attn_W2"], dtype=np.float32)
    w1e = np.asarray(inputs["L0_ffn_w1"], dtype=np.float32)
    w2e = np.asarray(inputs["L0_ffn_w2"], dtype=np.float32)
    W1p = np.ascontiguousarray(
        W1.reshape(8, 128, 16, 256).transpose(2, 1, 0, 3)).reshape(16, 128, 2048)
    W2p = np.ascontiguousarray(
        W2.reshape(32, 128, 8, 128).transpose(2, 1, 0, 3)).reshape(8, 128, 4096)
    w1t = w1e.transpose(0, 2, 1).reshape(NE, FC, 128, 2, FC, 128)
    w1tt = np.ascontiguousarray(w1t.transpose(0, 4, 2, 1, 3, 5)).reshape(NE, FC, 128, 2048)
    w2t = w2e.transpose(0, 2, 1).reshape(NE, FC, 128, FC, 128)
    w2tt = np.ascontiguousarray(w2t.transpose(0, 3, 2, 1, 4)).reshape(NE, FC, 128, 1024)
    return {"W1p": W1p, "W2p": W2p, "w1tt": w1tt, "w2tt": w2tt}


def _core_inputs(core, inputs, shared=None):
    if shared is None:
        shared = _prep_shared(inputs)
    src = inputs["L0_attn_src"]
    b, p = core // 4, core % 4
    lo = TOK * p - HALO
    x0 = np.zeros((HTOK, D), dtype=np.float32)
    s0, s1 = max(lo, 0), min(lo + HTOK, SEQ)
    x0[s0 - lo:s1 - lo] = np.asarray(src[b], dtype=np.float32)[s0:s1]
    m = np.arange(KT)[:, None, None]
    kt = np.arange(128)[None, :, None]
    qj = np.arange(640)[None, None, :]
    Ak = lo + 128 * m + kt
    q0m = np.maximum(128 * m - 256, Q0)
    q1m = np.minimum(128 * m + 384, Q0 + TOK)
    Aq = lo + q0m + qj
    valid = (np.abs(Ak - Aq) <= 256) & (Ak >= 0) & (Ak < SEQ) & (qj < (q1m - q0m))
    x0p = np.zeros((HTOK, PADW), dtype=np.float32)
    x0p.reshape(HTOK, H, 65)[:, :, 0:DH] = x0.reshape(HTOK, H, DH)
    x0p.reshape(HTOK, H, 65)[:, :, DH] = 1.0
    return {
        "x0": x0,
        "x0p": x0p,
        "x0f": np.ascontiguousarray(x0.T),
        "mski": valid.astype(ml_dtypes.bfloat16),
        "idbi": np.eye(128, dtype=np.float32).astype(ml_dtypes.bfloat16),
        "idfi": np.eye(128, dtype=np.float32),
        "W1p": shared["W1p"],
        "b1": np.asarray(inputs["L0_attn_b1"], dtype=np.float32),
        "W2p": shared["W2p"],
        "b2": np.asarray(inputs["L0_attn_b2"], dtype=np.float32),
        "g1": np.asarray(inputs["L0_attn_ln1_g"], dtype=np.float32),
        "c1": np.asarray(inputs["L0_attn_ln1_b"], dtype=np.float32),
        "g2": np.asarray(inputs["L0_attn_ln2_g"], dtype=np.float32),
        "c2": np.asarray(inputs["L0_attn_ln2_b"], dtype=np.float32),
        "Wg": np.asarray(inputs["L0_ffn_Wg"], dtype=np.float32),
        "bg": np.asarray(inputs["L0_ffn_bg"], dtype=np.float32),
        "w1tt": shared["w1tt"],
        "w2tt": shared["w2tt"],
    }


_CACHE = {}


def _get_nc(taps=()):
    key = tuple(sorted(taps))
    if key not in _CACHE:
        mk = MK(taps=key)
        _CACHE[key] = (mk.build(), mk.taps)
    return _CACHE[key][0]


def run_cores(inputs, taps=(), cores=None):
    nc = _get_nc(taps)
    cores = list(range(NCORE)) if cores is None else cores
    shared = _prep_shared(inputs)
    in_maps = [_core_inputs(c, inputs, shared) for c in cores]
    return run_bass_kernel_spmd(nc, in_maps, core_ids=cores)


def kernel(**inputs):
    res = run_cores(inputs)
    out = np.zeros((B, SEQ, D), dtype=np.float32)
    for c in range(NCORE):
        b, p = c // 4, c % 4
        out[b, TOK * p:TOK * (p + 1)] = res.results[c]["out"]
    return out


# version 25
# speedup vs baseline: 1.2053x; 1.2053x over previous
import numpy as np
import ml_dtypes
from contextlib import ExitStack

import concourse.bass as bass
import concourse.mybir as mybir
from concourse.bass_utils import run_bass_kernel_spmd

F32 = mybir.dt.float32
BF16 = mybir.dt.bfloat16
AF = mybir.ActivationFunctionType
ALU = mybir.AluOpType

NCORE = 8
B = 2
SEQ = 8192
TOK = 2048
HALO = 256
HTOK = TOK + 2 * HALO
D = 1024
NT = 16
FC = 8
H = 16
DH = 64
KT = 20
PADW = H * 65
DFF = 4096
NE = 8
DFFE = 2048
EPS = 1e-5
SHIFT = 80.0
Q0 = HALO


def _strip_range(m):
    q0 = max(128 * m - 256, Q0)
    q1 = min(128 * m + 384, Q0 + TOK)
    return q0, q1


class MK:
    def __init__(self, taps=()):
        self.taps = tuple(taps)
        nc = bass.Bass("TRN2", target_bir_lowering=False)
        self.nc = nc
        self.cT = self.cV = self.cS = self.cQ = self.cG = 0
        self.last_wait = {}
        self.pbf_rd = 0
        self.pf32_rd = 0
        self.out_toks = []

        dt = nc.dram_tensor
        self.x0 = dt("x0", [HTOK, D], F32, kind="ExternalInput").ap()
        self.x0p = dt("x0p", [HTOK, PADW], F32, kind="ExternalInput").ap()
        self.x0f = dt("x0f", [D, HTOK], F32, kind="ExternalInput").ap()
        self.mski = dt("mski", [KT, 128, 640], BF16, kind="ExternalInput").ap()
        self.idbi = dt("idbi", [128, 128], BF16, kind="ExternalInput").ap()
        self.idfi = dt("idfi", [128, 128], F32, kind="ExternalInput").ap()
        self.W1p = dt("W1p", [16, 128, 2048], F32, kind="ExternalInput").ap()
        self.b1 = dt("b1", [DFF], F32, kind="ExternalInput").ap()
        self.W2p = dt("W2p", [FC, 128, 4096], F32, kind="ExternalInput").ap()
        self.b2 = dt("b2", [D], F32, kind="ExternalInput").ap()
        self.g1 = dt("g1", [D], F32, kind="ExternalInput").ap()
        self.c1 = dt("c1", [D], F32, kind="ExternalInput").ap()
        self.g2 = dt("g2", [D], F32, kind="ExternalInput").ap()
        self.c2 = dt("c2", [D], F32, kind="ExternalInput").ap()
        self.Wg = dt("Wg", [NE, D], F32, kind="ExternalInput").ap()
        self.bg = dt("bg", [NE], F32, kind="ExternalInput").ap()
        self.w1tt = dt("w1tt", [NE, FC, 128, 2048], F32, kind="ExternalInput").ap()
        self.cdram = dt("cdram", [NE, TOK], BF16, kind="Internal").ap()
        self.w2tt = dt("w2tt", [NE, FC, 128, 1024], F32, kind="ExternalInput").ap()
        self.out = dt("out", [TOK, D], F32, kind="ExternalOutput").ap()
        self.tap_out = {}
        for t in self.taps:
            self.tap_out[t] = dt("tap_" + t, [TOK, D], F32, kind="ExternalOutput").ap()

        self.ctx = ExitStack()
        sem = lambda n: self.ctx.enter_context(nc.semaphore(n))
        self.sT = sem("sT")
        self.sV = sem("sV")
        self.sS = sem("sS")
        self.sGP = sem("sGP")
        self.cGP = 0
        self.NGL = 20
        self.NQL = 8
        self.gsems = [sem(f"g{i}") for i in range(self.NGL)]
        self.qsems = [sem(f"q{i}") for i in range(self.NQL)]
        self.gcnt = [0] * self.NGL
        self.qcnt = [0] * self.NQL
        self.grr = 0
        self.qrr = 0

        sbl = lambda n, shp, dtp: self.ctx.enter_context(
            nc.sbuf_tensor(n, shp, dtp, side="left"))
        self.x0s = sbl("x0s", [128, NT, D], F32)
        self.idb = sbl("idb", [128, 128], BF16)
        self.idf = sbl("idf", [128, 128], F32)
        self.b1s = sbl("b1s", [128, 32], F32)
        self.b2c = sbl("b2c", [128, FC], F32)
        self.WgT = sbl("WgT", [128, FC, NE], BF16)
        self.bgr = sbl("bgr", [128, NE], F32)
        self.logits = sbl("logits", [128, NT, NE], F32)
        self.cvals = sbl("cvals", [128, NT, NE], F32)
        self.stats = sbl("stats", [128, NT, 2], F32)
        self.negm = sbl("negm", [128, NT], F32)
        self.inv = sbl("inv", [128, NT], F32)
        self.bn6 = sbl("bn6", [128, 12], F32)
        self.gsc = sbl("gsc", [128, 4, NE], F32)
        self.gs1 = sbl("gs1", [128, 4], F32)
        self.gsc12 = sbl("gsc12", [128, NT, NE], F32)
        self.negm1 = sbl("negm1", [128, NT], F32)
        self.ge1 = sbl("ge1", [128, NT, NE], F32)
        self.gmk = sbl("gmk", [128, NT, NE], F32)
        self.gw8 = sbl("gw8", [128, NT, NE], F32)
        self.gs2 = sbl("gs2", [128, NT], F32)
        self.consts = sbl("consts", [128, 2], F32)
        nc.const_aps.register(F32, -SHIFT, self.consts[:, 0:1])
        nc.const_aps.register(F32, EPS, self.consts[:, 1:2])

    def tmark(self, ins):
        self.cT += 1
        ins.then_inc(self.sT, 1)
        return self.cT

    def vmark(self, ins):
        self.cV += 1
        ins.then_inc(self.sV, 1)
        return self.cV

    def smark(self, ins):
        self.cS += 1
        ins.then_inc(self.sS, 1)
        return self.cS

    def vop(self, name, *args, **kw):
        self.wait("vector", self.sV, self.cV)
        return self.vmark(getattr(self.nc.vector, name)(*args, **kw))

    def vop_nw(self, name, *args, **kw):
        return self.vmark(getattr(self.nc.vector, name)(*args, **kw))

    def sop(self, name, *args, **kw):
        self.wait("scalar", self.sS, self.cS)
        return self.smark(getattr(self.nc.scalar, name)(*args, **kw))

    def sop_nw(self, name, *args, **kw):
        return self.smark(getattr(self.nc.scalar, name)(*args, **kw))

    def gop_nw(self, name, *args, **kw):
        self.cGP += 1
        getattr(self.nc.gpsimd, name)(*args, **kw).then_inc(self.sGP, 1)
        return self.cGP

    def qdma(self, out, in_):
        i = self.qrr
        self.qrr = (self.qrr + 1) % self.NQL
        sm = self.qsems[i]
        self.wait("sync", sm, self.qcnt[i])
        self.qcnt[i] += 16
        self.nc.sync.dma_start(out=out, in_=in_).then_inc(sm, 16)
        return (sm, self.qcnt[i])

    def gdma(self, out, in_, accum=False):
        i = self.grr
        self.grr = (self.grr + 1) % self.NGL
        sm = self.gsems[i]
        self.wait("gpsimd", sm, self.gcnt[i])
        self.gcnt[i] += 16
        kw = {"accum_op": ALU.add} if accum else {}
        self.nc.gpsimd.dma_start(out=out, in_=in_, **kw).then_inc(sm, 16)
        return (sm, self.gcnt[i])

    def waitd(self, eng, tok):
        if isinstance(tok, list):
            for t in tok:
                self.waitd(eng, t)
            return
        if tok == 0 or tok is None:
            return
        sm, v = tok
        self.wait(eng, sm, v)

    def wait(self, eng, sem, val):
        if val is None or val <= 0:
            return
        key = (eng, sem.name if hasattr(sem, "name") else id(sem))
        if self.last_wait.get(key, -1) >= val:
            return
        self.last_wait[key] = val
        getattr(self.nc, eng).wait_ge(sem, val)

    def barrier(self):
        for e in ("tensor", "vector", "scalar", "sync", "gpsimd"):
            self.wait(e, self.sT, self.cT)
            self.wait(e, self.sV, self.cV)
            self.wait(e, self.sS, self.cS)
            self.wait(e, self.sGP, self.cGP)
            for i in range(self.NQL):
                self.wait(e, self.qsems[i], self.qcnt[i])
            for i in range(self.NGL):
                self.wait(e, self.gsems[i], self.gcnt[i])

    def pbf_write_guard(self):
        self.wait("tensor", self.sV, self.pbf_rd)

    def pf32_write_guard(self):
        self.wait("tensor", self.sV, self.pf32_rd)

    def phase0_loads(self):
        nc = self.nc
        self.m_idb = self.qdma(self.idb[:], self.idbi)
        m_id = self.qdma(self.idf[:], self.idfi)
        self.m_idf = m_id
        self.m_bgr = self.qdma(self.bgr[:], bass.AP(self.bg.tensor, 0, [[0, 128], [1, NE]]))
        self.m_bload = [
            self.qdma(self.b1n[0:32, :], bass.AP(self.b1.tensor, 0, [[128, 32], [1, 128]])),
            self.qdma(self.b2n[0:8, :], bass.AP(self.b2.tensor, 0, [[128, 8], [1, 128]]))]
        self.m_msk_a = self.qdma(
            self.msk[:, 0:6, :],
            bass.AP(self.mski.tensor, 0, [[640, 128], [128 * 640, 6], [1, 640]]))
        self.m_x0s_a = self.qdma(
            self.x0s[:, 0:8, :],
            bass.AP(self.x0.tensor, Q0 * D, [[D, 128], [128 * D, 8], [1, D]]))
        self.m_msk_b = self.qdma(
            self.msk[:, 6:KT, :],
            bass.AP(self.mski.tensor, 6 * 128 * 640,
                    [[640, 128], [128 * 640, KT - 6], [1, 640]]))
        self.m_x0s_b = self.qdma(
            self.x0s[:, 8:NT, :],
            bass.AP(self.x0.tensor, (Q0 + 8 * 128) * D, [[D, 128], [128 * D, 8], [1, D]]))
        self.m_x0s = [self.m_x0s_a, self.m_x0s_b]
        self.m_wgn = self.gdma(self.wgn[0:NE, :], self.Wg)
        self.m_x0fm = [None] * FC
        self.m_x0fm[0] = self.gdma(
            self.x0fm[:, 0, :],
            bass.AP(self.x0f.tensor, 0, [[HTOK, 128], [1, HTOK]]))
        self.m_x0tm = [self.gdma(
            bass.AP(self.x0tm, m * PADW, [[KT * PADW, 128], [1, PADW]]),
            bass.AP(self.x0p.tensor, m * 128 * PADW, [[PADW, 128], [1, PADW]]))
            for m in range(KT)]
        for fc in range(1, FC):
            self.m_x0fm[fc] = self.gdma(
                self.x0fm[:, fc, :],
                bass.AP(self.x0f.tensor, fc * 128 * HTOK, [[HTOK, 128], [1, HTOK]]))

        self.vop("memset", self.consts[:, 0:1], -SHIFT)
        self.m_ones = self.vop("memset", self.consts[:, 1:2], EPS)

        self.waitd("tensor", self.m_bload)
        self.waitd("tensor", m_id)
        self.waitd("tensor", self.m_idb)
        self.waitd("tensor", self.m_wgn)
        nc.tensor.transpose(self.pf32[:, 0:32], self.b1n[0:32, :], self.idf[0:32, 0:32])
        t1 = self.tmark(nc.tensor.transpose(self.pf32[:, 32:40], self.b2n[0:8, :],
                                            self.idf[0:8, 0:8]))
        for kc in range(FC):
            ins = nc.tensor.transpose(self.pbf[:, 8 * kc:8 * kc + 8],
                                      self.wgn[0:NE, 128 * kc:128 * kc + 128],
                                      self.idb[0:NE, 0:NE])
        t2 = self.tmark(ins)
        self.wait("vector", self.sT, t1)
        self.vop("tensor_copy", self.b1s[:], self.pf32[:, 0:32])
        self.pf32_rd = self.vop("tensor_copy", self.b2c[:], self.pf32[:, 32:40])
        self.m_bias_rd = self.pf32_rd
        self.wait("vector", self.sT, t2)
        self.m_params = self.vop("tensor_copy", 
            self.WgT[:], bass.AP(self.pbf, 0, [[1024, 128], [8, FC], [1, 8]]))
        self.pbf_rd = self.m_params

    def phase2_attn(self):
        nc = self.nc
        strip_last_pv = {}
        exp_mark = {}
        mask_mark = {}
        pv_mark = {}
        evac_mark = {}
        trans_mark = {}
        stt_mark = {}
        RING = 8
        LAG = 6
        self.wait("tensor", self.sV, self.m_params)
        gg = 0

        def do_pv(h, T):
            nonlocal gg
            grp = T // 4
            tloc = T % 4
            gb = h * 4 + grp
            self.wait("tensor", self.sGP, mask_mark[h * KT + T + 4])
            self.waitd("tensor", [self.m_x0tm[mm] for mm in range(T, T + 5)])
            ppv = self.ppv[grp % 2]
            if tloc == 0:
                self.wait("tensor", self.sV, evac_mark.get(gb - 2, 0))
            for i, mm in enumerate(range(T, T + 5)):
                sl = (h * KT + mm) % RING
                sq0, _ = _strip_range(mm)
                qc = (Q0 + 128 * T) - sq0
                ins = nc.tensor.matmul(
                    ppv[0:65, 128 * tloc:128 * tloc + 128],
                    self.x0tm[:, mm, 65 * h:65 * h + 65],
                    self.strb[:, sl, qc:qc + 128],
                    start=(i == 0), stop=(i == 4))
            pv_mark[gb] = pvm = self.tmark(ins)
            for mm in range(T, T + 5):
                k = h * KT + mm
                strip_last_pv[k] = max(strip_last_pv.get(k, 0), pvm)
            if tloc == 3:
                self.wait("vector", self.sT, pv_mark[gb])
                if gg >= 2:
                    self.wait("vector", self.sT, trans_mark.get(gg - 2, 0))
                evac_mark[gb] = self.vop_nw(
                    "tensor_copy", self.attnh[0:65, gg % 2, :], ppv[0:65, :])
                self.wait("tensor", self.sV, evac_mark[gb])
                if gg >= 2:
                    self.wait("tensor", self.sV, stt_mark.get(gg - 2, 0))
                ptm = self.ptm[gg % 2]
                for tl in range(4):
                    ins = nc.tensor.transpose(
                        ptm[:, 65 * tl:65 * tl + 65],
                        self.attnh[0:65, gg % 2, 128 * tl:128 * tl + 128],
                        self.idf[0:65, 0:65])
                trans_mark[gg] = self.tmark(ins)
                self.wait("vector", self.sT, trans_mark[gg])
                self.waitd("vector", self.m_x0s_a if grp < 2 else self.m_x0s_b)
                self.wait("vector", self.sV, stt_mark.get(gg - 1, 0))
                rm = self.vop_nw("reciprocal", self.invz[:, 0:4],
                                 bass.AP(ptm, 64, [[512, 128], [65, 4]]))
                self.wait("vector", self.sV, rm)
                for tl in range(4):
                    T2 = 4 * grp + tl
                    ins = self.vop_nw(
                        "scalar_tensor_tensor",
                        self.x0s[:, T2, 64 * h:64 * h + 64],
                        ptm[:, 65 * tl:65 * tl + 64],
                        self.invz[:, tl:tl + 1],
                        self.x0s[:, T2, 64 * h:64 * h + 64],
                        ALU.mult, ALU.add)
                stt_mark[gg] = ins
                gg += 1

        for h in range(H):
            f = h // 2
            self.waitd("tensor", self.m_x0fm[f])
            rows = slice(64 * (h % 2), 64 * (h % 2) + 64)
            for m in range(KT):
                gidx = h * KT + m
                slot = gidx % RING
                q0, q1 = _strip_range(m)
                w = q1 - q0
                ps = self.ps2[m % 2]
                if gidx >= 2:
                    self.wait("tensor", self.sS, exp_mark[gidx - 2])
                kap = self.x0fm[rows, f, 128 * m:128 * m + 128]
                c0 = 0
                while c0 < w:
                    n = min(512, w - c0)
                    ins = nc.tensor.matmul(
                        ps[:, c0:c0 + n], kap,
                        self.x0fm[rows, f, q0 + c0:q0 + c0 + n],
                        start=True, stop=True)
                    c0 += n
                qk_m = self.tmark(ins)
                self.wait("scalar", self.sV, self.m_ones)
                self.wait("scalar", self.sT, qk_m)
                if gidx >= RING:
                    self.wait("scalar", self.sT, strip_last_pv.get(gidx - RING, 0))
                    self.wait("scalar", self.sGP, mask_mark.get(gidx - RING, 0))
                exp_mark[gidx] = self.sop_nw(
                    "activation", self.strb[:, slot, 0:w], ps[:, 0:w],
                    AF.Exp, bias=-SHIFT, scale=1.0)
                self.waitd("gpsimd", self.m_msk_a if m < 6 else self.m_msk_b)
                self.wait("gpsimd", self.sS, exp_mark[gidx])
                if gidx >= RING:
                    self.wait("gpsimd", self.sT, strip_last_pv.get(gidx - RING, 0))
                mask_mark[gidx] = self.gop_nw(
                    "tensor_mul", self.strb[:, slot, 0:w],
                    self.strb[:, slot, 0:w], self.msk[:, m, 0:w])
                T = m - LAG
                if 0 <= T < NT:
                    do_pv(h, T)
            for m in range(KT, KT + LAG):
                T = m - LAG
                if 0 <= T < NT:
                    do_pv(h, T)

    def layernorm(self, gb=None, tiles=None):
        nc = self.nc
        tiles = list(range(NT)) if tiles is None else list(tiles)
        for t in tiles:
            self.vop("bn_stats", self.bn6[:, 0:6], self.x0s[:, t, 0:512])
            self.vop("bn_stats", self.bn6[:, 6:12], self.x0s[:, t, 512:1024])
            self.vop("bn_aggr", self.stats[:, t, :], self.bn6[:])
        t0, nt = tiles[0], len(tiles)
        m1 = self.vop("tensor_scalar_mul", 
            self.negm[:, t0:t0 + nt],
            bass.AP(self.stats, t0 * 2, [[NT * 2, 128], [2, nt]]), -1.0)
        self.wait("scalar", self.sV, m1)
        m2 = self.sop("activation", 
            self.inv[:, t0:t0 + nt],
            bass.AP(self.stats, t0 * 2 + 1, [[NT * 2, 128], [2, nt]]),
            AF.Sqrt, bias=EPS, scale=1.0)
        self.wait("vector", self.sS, m2)
        self.vop("reciprocal", self.inv[:, t0:t0 + nt], self.inv[:, t0:t0 + nt])
        for t in tiles:
            ins = self.vop("tensor_scalar", 
                self.x0s[:, t, :], self.x0s[:, t, :],
                self.negm[:, t:t + 1], self.inv[:, t:t + 1], ALU.add, ALU.mult)
            if gb is not None:
                self.vop("tensor_mul", self.x0s[:, t, :], self.x0s[:, t, :],
                                     self.gbr4[:, gb, :])
                ins = self.vop("tensor_add", self.x0s[:, t, :], self.x0s[:, t, :],
                                           self.gbr4[:, gb + 1, :])
        return ins

    def fm_build(self, dst, pfm, pfm_rd0=0, tiles=None, dep=0):
        nc = self.nc
        self.wait("tensor", self.sV, dep)
        pfm_rd = pfm_rd0
        last = 0
        for t in (range(NT) if tiles is None else tiles):
            for g2 in range(2):
                self.wait("tensor", self.sS, pfm_rd)
                for j4 in range(4):
                    j = 4 * g2 + j4
                    ins = nc.tensor.transpose(
                        pfm[:, 512 * g2 + 128 * j4:512 * g2 + 128 * j4 + 128],
                        self.x0s[:, t, 128 * j:128 * j + 128], self.idf[:])
                tm = self.tmark(ins)
                self.wait("scalar", self.sT, tm)
                ins2 = self.sop("activation",
                    bass.AP(dst, (4 * g2 * TOK + 128 * t),
                            [[FC * TOK, 128], [TOK, 4], [1, 128]]),
                    pfm[:, 512 * g2:512 * g2 + 512],
                    AF.Copy, bias=0.0, scale=1.0)
                pfm_rd = ins2
            last = pfm_rd
        return last

    def phase_ffn(self):
        nc = self.nc
        self.wait("scalar", self.sV, self.m_bias_rd)
        relu_mark = {}
        ff1_mark = {}
        ff2_mark = {}
        ev2_mark = {}
        flip_mark = {}
        add_mark = {}
        prev_w1_user = {}
        prev_w2_user = {}
        self.ln2m = {}
        self.u1m = {}
        mig_g = 0
        dc_g = 0
        for half in range(2):
            for grp in range(16):
                gslot = grp % 2
                self.wait("gpsimd", self.sT, prev_w1_user.get(gslot, 0))
                w1d = self.gdma(
                    self.w1buf[:, gslot, :, :],
                    bass.AP(self.W1p.tensor, grp * 128 * 2048,
                            [[2048, 128], [1, 2048]]))
                self.waitd("tensor", w1d)
                for mi in range(2):
                    hb = self.hb[mig_g % 2]
                    if mig_g >= 2:
                        self.wait("tensor", self.sS, relu_mark[mig_g - 2])
                    mig = 2 * grp + mi
                    for nb in range(2):
                        self.wait("tensor", self.sS, self.ybf_half[2 * half + nb])
                        for ki in range(FC):
                            ins = nc.tensor.matmul(
                                hb[:, 512 * nb:512 * nb + 512],
                                self.w1buf[:, gslot, ki, 128 * mi:128 * mi + 128],
                                self.ybf[:, ki,
                                         1024 * half + 512 * nb:1024 * half + 512 * nb + 512],
                                start=(ki == 0), stop=(ki == FC - 1))
                    mmk = self.tmark(ins)
                    self.wait("scalar", self.sT, mmk)
                    relu_mark[mig_g] = self.sop_nw("activation", 
                        self.hsb[:, mig, :], hb[:, 0:1024], AF.Relu,
                        bias=self.b1s[:, mig:mig + 1], scale=1.0)
                    mig_g += 1
                prev_w1_user[gslot] = self.cT
            if half == 1:
                self.outer_ln_half(0)
            for dc in range(FC):
                dslot = dc % 2
                self.wait("gpsimd", self.sT, prev_w2_user.get(dslot, 0))
                w2d = self.gdma(
                    self.w2buf[:, dslot, :, :],
                    bass.AP(self.W2p.tensor, dc * 128 * 4096,
                            [[4096, 128], [1, 4096]]))
                fb = self.hb[dslot]
                self.waitd("tensor", w2d)
                self.wait("tensor", self.sS, relu_mark[mig_g - 1])
                if dc_g >= 2:
                    self.wait("tensor", self.sS, ev2_mark[dc_g - 2])
                elif half == 0 and dc < 2:
                    self.wait("tensor", self.sS, relu_mark[mig_g - 1])
                for nb in range(2):
                    for ji in range(32):
                        ins = nc.tensor.matmul(
                            fb[:, 512 * nb:512 * nb + 512],
                            self.w2buf[:, dslot, ji, :],
                            self.hsb[:, ji, 512 * nb:512 * nb + 512],
                            start=(ji == 0), stop=(ji == 31))
                ff2_mark[dc_g] = mmk = self.tmark(ins)
                prev_w2_user[dslot] = mmk
                self.wait("scalar", self.sT, mmk)
                if dc_g >= 1:
                    self.wait("scalar", self.sT, flip_mark[dc_g - 1])
                ev2_mark[dc_g] = self.sop_nw("activation", 
                    self.ffsb[:, 0, :], fb[:, 0:1024], AF.Identity,
                    bias=self.b2c[:, dc:dc + 1], scale=1.0)
                self.wait("tensor", self.sS, ev2_mark[dc_g])
                self.pbf_write_guard()
                for tl in range(8):
                    ins = nc.tensor.transpose(
                        self.pbf[:, 128 * tl:128 * tl + 128],
                        self.ffsb[:, 0, 128 * tl:128 * tl + 128], self.idb[:])
                flip_mark[dc_g] = self.tmark(ins)
                self.wait("vector", self.sT, flip_mark[dc_g])
                xap = bass.AP(self.x0s, (8 * half) * D + 128 * dc,
                              [[NT * D, 128], [D, 8], [1, 128]])
                add_mark[dc_g] = self.vop("tensor_add", 
                    xap, bass.AP(self.pbf, 0, [[1024, 128], [128, 8], [1, 128]]), xap)
                self.pbf_rd = add_mark[dc_g]
                dc_g += 1
            self.waitd("vector", self.mg_tok)
            self.ln2m[half] = self.layernorm(gb=2, tiles=range(8 * half, 8 * half + 8))

    def outer_ln_half(self, half):
        self.wait("gpsimd", self.sV, self.ln2m[half])
        xr = self.gdma(
            self.x0s[:, 8 * half:8 * half + 8, :],
            bass.AP(self.x0.tensor, (Q0 + 1024 * half) * D,
                    [[D, 128], [128 * D, 8], [1, D]]),
            accum=True)
        self.waitd("vector", xr)
        self.u1m[half] = self.layernorm(gb=None, tiles=range(8 * half, 8 * half + 8))

    def phase_gating(self):
        nc = self.nc
        self.wait("tensor", self.sS, self.m_ubf)
        self.wait("tensor", self.sV, self.m_params)
        self.pf32_write_guard()
        for t in range(NT):
            for f in range(FC):
                ins = nc.tensor.matmul(
                    self.pf32[:, 8 * t:8 * t + 8],
                    self.ubf[:, f, 128 * t:128 * t + 128],
                    self.WgT[:, f, :],
                    start=(f == 0), stop=(f == FC - 1))
        lg = self.tmark(ins)
        self.wait("vector", self.sT, lg)
        self.waitd("vector", self.m_bgr)
        self.pf32_rd = self.vop("tensor_add", 
            bass.AP(self.logits, 0, [[NT * NE, 128], [NE, NT], [1, NE]]),
            bass.AP(self.pf32, 0, [[512, 128], [NE, NT], [1, NE]]),
            bass.AP(self.bgr, 0, [[NE, 128], [0, NT], [1, NE]]))
        self.wait("vector", self.sV, self.cV)
        for t in range(NT):
            self.vop_nw("max", self.gsc12[:, t, :], self.logits[:, t, :])
        mv = self.vop("tensor_scalar_mul",
            self.negm1[:], bass.AP(self.gsc12, 0, [[NT * NE, 128], [NE, NT]]), -1.0)
        self.wait("scalar", self.sV, mv)
        se = 0
        for t in range(NT):
            se = self.sop_nw("activation",
                self.ge1[:, t, :], self.logits[:, t, :], AF.Exp,
                bias=self.negm1[:, t:t + 1], scale=1.0)
        self.wait("vector", self.sS, se)
        for t in range(NT):
            self.vop_nw("tensor_single_scalar",
                self.gmk[:, t, :], self.logits[:, t, :],
                self.gsc12[:, t, 1:2], ALU.is_ge)
        self.wait("vector", self.sV, self.cV)
        for t in range(NT):
            self.vop_nw("scalar_tensor_tensor",
                self.gw8[:, t, :], self.ge1[:, t, :], 1.0, self.gmk[:, t, :],
                ALU.mult, ALU.mult, accum_out=self.gs2[:, t:t + 1])
        iv = self.vop("reciprocal", self.gs2[:], self.gs2[:])
        self.wait("vector", self.sV, iv)
        ins = 0
        for t in range(NT):
            ins = self.vop_nw("tensor_scalar",
                self.cvals[:, t, :], self.gw8[:, t, :],
                self.gs2[:, t:t + 1], None, ALU.mult)
        self.m_gate = ins
        self.wait("tensor", self.sV, self.m_gate)
        for qt in range(4):
            self.pf32_write_guard()
            for tl in range(4):
                t = 4 * qt + tl
                ins = nc.tensor.transpose(
                    self.pf32[0:NE, 128 * tl:128 * tl + 128],
                    bass.AP(self.cvals, t * NE, [[NT * NE, 128], [1, NE]]),
                    self.idf[:])
            tm = self.tmark(ins)
            self.wait("vector", self.sT, tm)
            ins = self.vop("tensor_copy", self.ctsb[0:NE, 512 * qt:512 * qt + 512],
                                        self.pf32[0:NE, 0:512])
            self.pf32_rd = ins
        self.m_ct = self.pf32_rd
        self.wait("gpsimd", self.sV, self.m_ct)
        self.m_cdram = self.gdma(self.cdram, self.ctsb[0:NE, :])

    def phase_moe(self):
        nc = self.nc
        W1SLOTS = 4
        silu_mark = {}
        amul_mark = {}
        hmm_mark = {}
        w1slot_user = {}
        w1n_user = {}
        w2n_user = {}
        w2t_user = {}
        oflip_prev = 0
        ob_user = {}
        self.wait("tensor", self.sS, self.m_ubf)

        def stageA_load(pair, eidx, e, g):
            slot2 = g % 2
            self.wait("gpsimd", self.sT, w1slot_user.get(slot2, 0))
            return self.gdma(
                bass.AP(self.w1T, slot2 * 2048, [[2 * 2048, 128], [1, 2048]]),
                bass.AP(self.w1tt.tensor, (e * FC + g) * 128 * 2048,
                        [[2048, 128], [1, 2048]]))

        def stageA_units(pair, eidx, e, g, w1d, crep_d):
            slot2 = g % 2
            self.waitd("tensor", w1d)
            for tq in range(4):
                qsl = tq % 2
                self.wait("tensor", self.sS,
                          silu_mark.get((e, g, tq - 2), silu_mark.get(("prev", qsl), 0)))
                self.wait("tensor", self.sV,
                          amul_mark.get((e, g, tq - 2), amul_mark.get(("prevu", qsl), 0)))
                for hu in range(2):
                    hps = self.hg[qsl] if hu == 0 else self.hu[qsl]
                    for kc in range(FC):
                        ins = nc.tensor.matmul(
                            hps[:],
                            bass.AP(self.w1T, slot2 * 2048 + kc * 256 + hu * 128,
                                    [[2 * 2048, 128], [1, 128]]),
                            self.ubf[:, kc, 512 * tq:512 * tq + 512],
                            start=(kc == 0), stop=(kc == FC - 1))
                hm = self.tmark(ins)
                hmm_mark[(e, g, tq)] = hm
                w1slot_user[slot2] = hm
                self.wait("scalar", self.sT, hm)
                self.wait("scalar", self.sV,
                          amul_mark.get((e, g, tq - 2), amul_mark.get(("prevu", qsl), 0)))
                sm = self.sop_nw("activation",
                    self.sg[:, qsl, :], self.hg[qsl][:], AF.Silu,
                    bias=0.0, scale=1.0)
                silu_mark[(e, g, tq)] = sm
                silu_mark[("prev", qsl)] = sm
                self.wait("vector", self.sS, sm)
                self.waitd("vector", crep_d)
                self.vop("tensor_mul",
                    self.sg[:, qsl, :], self.sg[:, qsl, :],
                    self.crep[:, eidx, 512 * tq:512 * tq + 512])
                am = self.vop("tensor_mul",
                    self.a[:, eidx, g, 512 * tq:512 * tq + 512],
                    self.sg[:, qsl, :], self.hu[qsl][:])
                amul_mark[(e, g, tq)] = am
                amul_mark[("prevu", qsl)] = am
            return am

        for pair in range(4):
            for eidx in range(2):
                e = 2 * pair + eidx
                ev = {0: stageA_load(pair, eidx, e, 0)}
                ev[1] = stageA_load(pair, eidx, e, 1)
                self.waitd("gpsimd", self.m_cdram)
                self.wait("gpsimd", self.sV, amul_mark.get(("ce", pair - 1, eidx), 0))
                crep_d = self.gdma(
                    self.crep[:, eidx, :],
                    bass.AP(self.cdram.tensor, e * TOK, [[0, 128], [1, TOK]]))
                for g in range(1, FC + 1):
                    if 2 <= g < FC:
                        ev[g] = stageA_load(pair, eidx, e, g)
                    am = stageA_units(pair, eidx, e, g - 1, ev[g - 1], crep_d)
                amul_mark[("ce", pair, eidx)] = am
            a_ready = max(amul_mark[(2 * pair + i, FC - 1, tq)]
                          for i in range(2) for tq in range(4))
            for half in range(2):
                for dc in range(FC):
                    dsl = dc % 2
                    w2toks = []
                    for eidx in range(2):
                        e = 2 * pair + eidx
                        self.wait("gpsimd", self.sT, w2t_user.get(dsl, 0))
                        w2toks.append(self.gdma(
                            self.w2t[:, dsl, eidx, :],
                            bass.AP(self.w2tt.tensor, (e * FC + dc) * 128 * 1024,
                                    [[1024, 128], [1, 1024]])))
                    self.waitd("tensor", w2toks)
                    self.wait("tensor", self.sV, a_ready)
                    self.wait("tensor", self.sS, ob_user.get(0, 0))
                    obp = self.obs[0]
                    for nb in range(2):
                        first = True
                        for eidx in range(2):
                            for jc in range(FC):
                                ins = nc.tensor.matmul(
                                    obp[:, 512 * nb:512 * nb + 512],
                                    self.w2t[:, dsl, eidx, 128 * jc:128 * jc + 128],
                                    self.a[:, eidx, jc,
                                           1024 * half + 512 * nb:1024 * half + 512 * nb + 512],
                                    start=first,
                                    stop=(eidx == 1 and jc == FC - 1))
                                first = False
                    om = self.tmark(ins)
                    w2t_user[dsl] = om
                    self.wait("scalar", self.sT, om)
                    self.wait("scalar", self.sT, oflip_prev)
                    oev = self.sop_nw("activation",
                        self.osb[:], obp[:, 0:1024], AF.Copy, bias=0.0, scale=1.0)
                    ob_user[0] = oev
                    self.wait("tensor", self.sS, oev)
                    self.pbf_write_guard()
                    for tl in range(8):
                        ins = nc.tensor.transpose(
                            self.pbf[:, 128 * tl:128 * tl + 128],
                            self.osb[:, 128 * tl:128 * tl + 128], self.idb[:])
                    ofl = self.tmark(ins)
                    oflip_prev = ofl
                    self.wait("vector", self.sT, ofl)
                    xap = bass.AP(self.x0s, (8 * half) * D + 128 * dc,
                                  [[NT * D, 128], [D, 8], [1, 128]])
                    self.pbf_rd = self.vop("tensor_add",
                        xap, bass.AP(self.pbf, 0, [[1024, 128], [128, 8], [1, 128]]),
                        xap)
                    if pair == 3 and dc == FC - 1 and "accpre" not in self.taps:
                        lnm = self.layernorm(gb=None, tiles=range(8 * half, 8 * half + 8))
                        self.wait("sync", self.sV, lnm)
                        mo = self.qdma(
                            bass.AP(self.out.tensor, half * 8 * 128 * D,
                                    [[D, 128], [128 * D, 8], [1, D]]),
                            self.x0s[:, 8 * half:8 * half + 8, :])
                        self.out_toks.append(mo)

    def tap(self, name):
        if name in self.taps:
            self.barrier()
            m = self.qdma(
                bass.AP(self.tap_out[name].tensor, 0, [[D, 128], [128 * D, NT], [1, D]]),
                self.x0s[:])
            self.waitd("sync", m)

    def build(self):
        nc = self.nc
        with self.ctx:
            attn_ctx = ExitStack()
            r = lambda n, shp, dtp: attn_ctx.enter_context(
                nc.sbuf_tensor(n, shp, dtp, side="right"))
            self.msk = r("msk", [128, KT, 640], BF16)
            self.x0tm = r("x0tm", [128, KT, PADW], BF16)
            self.x0fm = r("x0fm", [128, FC, HTOK], BF16)
            self.strb = r("strb", [128, 8, 640], BF16)
            self.attnh = r("attnh", [128, 2, 512], F32)
            self.invz = r("invz", [128, 4], F32)
            self.b1n = r("b1n", [128, 128], F32)
            self.b2n = r("b2n", [128, 128], F32)
            self.wgn = r("wgn", [128, D], BF16)

            p0_ctx = ExitStack()
            self.pbf = p0_ctx.enter_context(nc.psum_tensor("pbf0", [128, 1024], BF16))
            self.pf32 = p0_ctx.enter_context(nc.psum_tensor("pf0", [128, 512], F32))
            self.phase0_loads()
            p0_ctx.close()

            pa_ctx = ExitStack()
            pp = lambda n, shp: pa_ctx.enter_context(nc.psum_tensor(n, shp, F32))
            self.ps2 = [pp("ps0", [128, 640]), pp("ps1", [128, 640])]
            self.ppv = [pp("ppv0", [128, 512]), pp("ppv1", [128, 512])]
            self.ptm = [pp("ptm0", [128, 512]), pp("ptm1", [128, 512])]
            self.phase2_attn()
            self.tap("s")
            self.barrier()
            pa_ctx.close()
            self.tap("y")
            self.barrier()
            attn_ctx.close()

            gb_ctx = ExitStack()
            self.gbr4 = gb_ctx.enter_context(
                nc.sbuf_tensor("gbr4", [128, 4, D], F32, side="right"))
            ffn_ctx = ExitStack()
            r2 = lambda n, shp, dtp: ffn_ctx.enter_context(
                nc.sbuf_tensor(n, shp, dtp, side="right"))
            gtoks = [self.qdma(self.gbr4[:, i, :],
                            bass.AP(src.tensor, 0, [[0, 128], [1, D]]))
                  for i, src in enumerate((self.g1, self.c1, self.g2, self.c2))]
            self.mg_tok = gtoks
            self.waitd("vector", gtoks[:2])
            self.ybf = r2("ybf", [128, FC, TOK], BF16)
            self.hsb = r2("hsb", [128, 32, 1024], BF16)
            self.w1buf = r2("w1buf", [128, 2, 8, 256], BF16)
            self.w2buf = r2("w2buf", [128, 2, 32, 128], BF16)
            self.ffsb = r2("ffsb", [128, 1, 1024], BF16)
            pf_ctx = ExitStack()
            self.pbf = pf_ctx.enter_context(nc.psum_tensor("pbf2", [128, 1024], BF16))
            self.hb = [pf_ctx.enter_context(nc.psum_tensor("hb0", [128, 1024], F32)),
                       pf_ctx.enter_context(nc.psum_tensor("hb1", [128, 1024], F32))]
            pfmb = pf_ctx.enter_context(nc.psum_tensor("pfmb", [128, 1024], F32))
            self.pbf_rd = 0
            self.ybf_half = {}
            for qd in range(4):
                lnm = self.layernorm(gb=0, tiles=range(4 * qd, 4 * qd + 4))
                self.ybf_half[qd] = self.fm_build(
                    self.ybf, pfmb, tiles=range(4 * qd, 4 * qd + 4), dep=lnm)
            self.m_ybf = self.ybf_half[3]
            self.phase_ffn()
            self.barrier()
            pf_ctx.close()
            self.tap("zpre")
            ffn_ctx.close()
            self.tap("u1")
            self.barrier()
            gb_ctx.close()

            moe_ctx = ExitStack()
            r3 = lambda n, shp, dtp: moe_ctx.enter_context(
                nc.sbuf_tensor(n, shp, dtp, side="right"))
            self.ubf = r3("ubf", [128, FC, TOK], BF16)
            self.a = r3("a", [128, 2, FC, TOK], BF16)
            self.w1T = r3("w1T", [128, 2 * 2048], BF16)
            self.w2t = r3("w2t", [128, 2, 2, 1024], BF16)
            self.sg = r3("sg", [128, 2, 512], BF16)
            self.crep = r3("crep", [128, 2, TOK], BF16)
            self.ctsb = r3("ctsb", [128, TOK], BF16)
            self.osb = r3("osb", [128, 1024], BF16)
            pm_ctx = ExitStack()
            self.pbf = pm_ctx.enter_context(nc.psum_tensor("pbf3", [128, 1024], BF16))
            self.hg = [pm_ctx.enter_context(nc.psum_tensor("hg0", [128, 512], F32)),
                       pm_ctx.enter_context(nc.psum_tensor("hg1", [128, 512], F32))]
            self.hu = [pm_ctx.enter_context(nc.psum_tensor("hu0", [128, 512], F32)),
                       pm_ctx.enter_context(nc.psum_tensor("hu1", [128, 512], F32))]
            self.obs = [pm_ctx.enter_context(nc.psum_tensor("ob", [128, 1024], F32))]
            pg_ctx = ExitStack()
            self.pf32 = pg_ctx.enter_context(nc.psum_tensor("pf3", [128, 512], F32))
            self.pbf_rd = 0
            self.pf32_rd = 0
            self.outer_ln_half(1)
            ubf_q = {}
            for qd in range(4):
                ubf_q[qd] = self.fm_build(
                    self.ubf, self.obs[0], tiles=range(4 * qd, 4 * qd + 4),
                    dep=self.u1m[qd // 2])
            self.m_ubf = ubf_q[3]
            self.phase_gating()
            self.barrier()
            pg_ctx.close()
            self.obs.append(pm_ctx.enter_context(nc.psum_tensor("ob2", [128, 512], F32)))
            self.phase_moe()
            if "accpre" in self.taps:
                self.barrier()
                self.tap("accpre")
                self.layernorm(gb=None)
                m = self.qdma(
                    bass.AP(self.out.tensor, 0, [[D, 128], [128 * D, NT], [1, D]]),
                    self.x0s[:])
                self.out_toks.append(m)
            self.barrier()
            pm_ctx.close()
            moe_ctx.close()
            self.waitd("sync", self.out_toks)
        return nc


def _prep_shared(inputs):
    W1 = np.asarray(inputs["L0_attn_W1"], dtype=np.float32)
    W2 = np.asarray(inputs["L0_attn_W2"], dtype=np.float32)
    w1e = np.asarray(inputs["L0_ffn_w1"], dtype=np.float32)
    w2e = np.asarray(inputs["L0_ffn_w2"], dtype=np.float32)
    W1p = np.ascontiguousarray(
        W1.reshape(8, 128, 16, 256).transpose(2, 1, 0, 3)).reshape(16, 128, 2048)
    W2p = np.ascontiguousarray(
        W2.reshape(32, 128, 8, 128).transpose(2, 1, 0, 3)).reshape(8, 128, 4096)
    w1t = w1e.transpose(0, 2, 1).reshape(NE, FC, 128, 2, FC, 128)
    w1tt = np.ascontiguousarray(w1t.transpose(0, 4, 2, 1, 3, 5)).reshape(NE, FC, 128, 2048)
    w2t = w2e.transpose(0, 2, 1).reshape(NE, FC, 128, FC, 128)
    w2tt = np.ascontiguousarray(w2t.transpose(0, 3, 2, 1, 4)).reshape(NE, FC, 128, 1024)
    return {"W1p": W1p, "W2p": W2p, "w1tt": w1tt, "w2tt": w2tt}


def _core_inputs(core, inputs, shared=None):
    if shared is None:
        shared = _prep_shared(inputs)
    src = inputs["L0_attn_src"]
    b, p = core // 4, core % 4
    lo = TOK * p - HALO
    x0 = np.zeros((HTOK, D), dtype=np.float32)
    s0, s1 = max(lo, 0), min(lo + HTOK, SEQ)
    x0[s0 - lo:s1 - lo] = np.asarray(src[b], dtype=np.float32)[s0:s1]
    m = np.arange(KT)[:, None, None]
    kt = np.arange(128)[None, :, None]
    qj = np.arange(640)[None, None, :]
    Ak = lo + 128 * m + kt
    q0m = np.maximum(128 * m - 256, Q0)
    q1m = np.minimum(128 * m + 384, Q0 + TOK)
    Aq = lo + q0m + qj
    valid = (np.abs(Ak - Aq) <= 256) & (Ak >= 0) & (Ak < SEQ) & (qj < (q1m - q0m))
    x0p = np.zeros((HTOK, PADW), dtype=np.float32)
    x0p.reshape(HTOK, H, 65)[:, :, 0:DH] = x0.reshape(HTOK, H, DH)
    x0p.reshape(HTOK, H, 65)[:, :, DH] = 1.0
    return {
        "x0": x0,
        "x0p": x0p,
        "x0f": np.ascontiguousarray(x0.T),
        "mski": valid.astype(ml_dtypes.bfloat16),
        "idbi": np.eye(128, dtype=np.float32).astype(ml_dtypes.bfloat16),
        "idfi": np.eye(128, dtype=np.float32),
        "W1p": shared["W1p"],
        "b1": np.asarray(inputs["L0_attn_b1"], dtype=np.float32),
        "W2p": shared["W2p"],
        "b2": np.asarray(inputs["L0_attn_b2"], dtype=np.float32),
        "g1": np.asarray(inputs["L0_attn_ln1_g"], dtype=np.float32),
        "c1": np.asarray(inputs["L0_attn_ln1_b"], dtype=np.float32),
        "g2": np.asarray(inputs["L0_attn_ln2_g"], dtype=np.float32),
        "c2": np.asarray(inputs["L0_attn_ln2_b"], dtype=np.float32),
        "Wg": np.asarray(inputs["L0_ffn_Wg"], dtype=np.float32),
        "bg": np.asarray(inputs["L0_ffn_bg"], dtype=np.float32),
        "w1tt": shared["w1tt"],
        "w2tt": shared["w2tt"],
    }


_CACHE = {}


def _get_nc(taps=()):
    key = tuple(sorted(taps))
    if key not in _CACHE:
        mk = MK(taps=key)
        _CACHE[key] = (mk.build(), mk.taps)
    return _CACHE[key][0]


def run_cores(inputs, taps=(), cores=None):
    nc = _get_nc(taps)
    cores = list(range(NCORE)) if cores is None else cores
    shared = _prep_shared(inputs)
    in_maps = [_core_inputs(c, inputs, shared) for c in cores]
    return run_bass_kernel_spmd(nc, in_maps, core_ids=cores)


def kernel(**inputs):
    res = run_cores(inputs)
    out = np.zeros((B, SEQ, D), dtype=np.float32)
    for c in range(NCORE):
        b, p = c // 4, c % 4
        out[b, TOK * p:TOK * (p + 1)] = res.results[c]["out"]
    return out
